# version 01
# baseline (speedup 1.0000x reference)
import math
from contextlib import ExitStack

import numpy as np

import concourse.bass as bass
import concourse.mybir as mybir
from concourse.bass_utils import run_bass_kernel_spmd

F32 = mybir.dt.float32
BF16 = mybir.dt.bfloat16
AF = mybir.ActivationFunctionType
OP = mybir.AluOpType
AX = mybir.AxisListType

S = 2048
D = 2048
NDC = 16
TT = 512
NTT = 4

HEAP = 38912
A0 = 0
WK0, WK1 = 32768, 34816
GEN = 36864
R_LNM, R_LNR, R_B, R_C = 0, 1, 2, 3
CSC = 8192

NF4 = [-1.0, -0.6962, -0.5251, -0.3949, -0.2844, -0.1848, -0.0911, 0.0,
       0.0796, 0.1609, 0.2461, 0.3379, 0.4407, 0.5626, 0.723, 1.0]
NF4_THR = [(NF4[i] + NF4[i + 1]) / 2.0 for i in range(15)]
NF4_DEL = [NF4[i + 1] - NF4[i] for i in range(15)]


def pack_slabs(w):
    K, F = w.shape
    KO, FO = K // 128, F // 128
    t = w.reshape(KO, 128, FO, 128)
    return np.ascontiguousarray(t.transpose(2, 1, 0, 3).reshape(FO, 128, KO * 128))


def pack_gamma(g):
    return np.ascontiguousarray(np.asarray(g).reshape(-1, 128).T)


def rope_neox(dh, s, theta=10000.0):
    half = dh // 2
    inv = (1.0 / theta ** (np.arange(0, dh, 2, dtype=np.float32) / np.float32(dh))).astype(np.float32)
    t = np.arange(s, dtype=np.float32)
    fr = np.outer(t, inv).astype(np.float32)
    cos, sin = np.cos(fr).astype(np.float32), np.sin(fr).astype(np.float32)
    cosT = np.ascontiguousarray(np.concatenate([cos, cos], 1).T)
    sinS = np.ascontiguousarray(np.concatenate([sin, sin], 1).T)
    sinS[:half] *= -1.0
    P = np.zeros((dh, dh), np.float32)
    for i in range(half):
        P[half + i, i] = 1.0
        P[i, half + i] = 1.0
    return cosT, sinS, P


def rope_inter(dh, s, theta=10000.0):
    half = dh // 2
    th = (1.0 / theta ** (2.0 * np.arange(half, dtype=np.float32) / np.float32(dh))).astype(np.float32)
    m = np.arange(s, dtype=np.float32)
    fr = np.outer(m, th).astype(np.float32)
    cos, sin = np.cos(fr).astype(np.float32), np.sin(fr).astype(np.float32)
    cosT = np.zeros((dh, s), np.float32)
    sinS = np.zeros((dh, s), np.float32)
    cosT[0::2] = cos.T
    cosT[1::2] = cos.T
    sinS[0::2] = -sin.T
    sinS[1::2] = sin.T
    P = np.zeros((dh, dh), np.float32)
    for p in range(half):
        P[2 * p + 1, 2 * p] = 1.0
        P[2 * p, 2 * p + 1] = 1.0
    return cosT, sinS, P


def _np2dt(d):
    d = np.dtype(d)
    if d == np.float32:
        return F32
    if d.name == "bfloat16":
        return BF16
    raise ValueError(d)




_NOAUTO = {"wait_ge", "sem_inc", "dma_start", "dma_start_transpose", "wait_op",
           "nop", "engine_nop", "drain", "halt"}


class EngProxy:
    def __init__(self, b, eng, key):
        self._b = b
        self._e = eng
        self._k = key

    def __getattr__(self, name):
        f = getattr(self._e, name)
        if name in _NOAUTO or not callable(f):
            return f
        b, k = self._b, self._k

        def wrap(*a, **kw):
            if b.cnt[k] > 0:
                self._e.wait_ge(b.sem[k], b.cnt[k])
            inst = f(*a, **kw)
            try:
                inst.then_inc(b.sem[k], 1)
                b.cnt[k] += 1
            except Exception:
                pass
            return inst
        return wrap


class Bld:
    def __init__(self):
        nc = bass.Bass("TRN2", target_bir_lowering=False)
        self.nc = nc
        self.ctx = ExitStack()
        self.inputs = {}
        self.per_core = {}
        self.sem = {}
        self.cnt = {}
        for k in ("T", "V", "S", "G"):
            self.sem[k] = self.ctx.enter_context(nc.semaphore("sem" + k))
            self.cnt[k] = 0
        for q in ("DY", "DG", "DS"):
            for j in range(4):
                k = q + str(j)
                self.sem[k] = self.ctx.enter_context(nc.semaphore("sem" + k))
                self.cnt[k] = 0
        self.drr = {"DY": 0, "DG": 0, "DS": 0}
        self.T = EngProxy(self, nc.tensor, "T")
        self.V = EngProxy(self, nc.vector, "V")
        self.SC = EngProxy(self, nc.scalar, "S")
        self.G = EngProxy(self, nc.gpsimd, "G")
        self.Y = nc.sync
        self.psum = [self.ctx.enter_context(nc.psum_tensor(f"ps{i}", [128, 512], F32))
                     for i in range(8)]
        self.ps_ev = [None] * 8
        self.ps_free = list(range(8))
        self.ps_rr = 0

    def eng(self, k):
        m = {"T": self.T, "V": self.V, "S": self.SC, "G": self.G,
             "DY": self.Y, "DG": self.G, "DS": self.SC}
        return m[k[:2]] if k[0] == "D" else m[k]

    def inc(self, k, inst):
        if k[0] == "D":
            inst.then_inc(self.sem[k], 16)
            self.cnt[k] += 16
        return (k, self.cnt[k])

    def ev(self, k):
        return (k, self.cnt[k])

    def wait(self, ek, ev):
        if ev is None:
            return
        k, v = ev
        if v <= 0:
            return
        self.eng(ek).wait_ge(self.sem[k], v)

    def tick(self, k):
        return (k, self.cnt[k])

    def barrier(self):
        evs = [(k, self.cnt[k]) for k in "TVSG"]
        for q in ("DY", "DG", "DS"):
            for j in range(4):
                k = q + str(j)
                if self.cnt[k] > 0:
                    evs.append((k, self.cnt[k]))
        for c in ("T", "V", "S", "G", "DY"):
            for e in evs:
                if e[0] != c:
                    self.wait(c, e)

    def sb(self, name, shape, dtype):
        return self.ctx.enter_context(self.nc.sbuf_tensor(name, list(shape), dtype))

    def dram(self, name, shape, dtype):
        return self.nc.dram_tensor(name, list(shape), dtype, kind="Internal").ap()

    def inp(self, name, arr):
        a = np.ascontiguousarray(arr)
        t = self.nc.dram_tensor(name, list(a.shape), _np2dt(a.dtype), kind="ExternalInput")
        self.inputs[name] = a
        return t.ap()

    def inp_pc(self, name, arrs):
        a0 = np.ascontiguousarray(arrs[0])
        t = self.nc.dram_tensor(name, list(a0.shape), _np2dt(a0.dtype), kind="ExternalInput")
        self.per_core[name] = [np.ascontiguousarray(a) for a in arrs]
        return t.ap()

    def dma(self, out, in_, q="DY"):
        j = self.drr[q] % 4
        self.drr[q] += 1
        k = q + str(j)
        if self.cnt[k] > 0:
            self.eng(q).wait_ge(self.sem[k], self.cnt[k])
        inst = self.eng(q).dma_start(out=out, in_=in_)
        return self.inc(k, inst)

    def dwait(self, ek, q="DY"):
        for j in range(4):
            k = q + str(j)
            if self.cnt[k] > 0:
                self.eng(ek).wait_ge(self.sem[k], self.cnt[k])

    def ps_get(self):
        i = self.ps_free[self.ps_rr % len(self.ps_free)]
        self.ps_rr += 1
        if self.ps_ev[i] is not None:
            self.wait("T", self.ps_ev[i])
            self.ps_ev[i] = None
        return i, self.psum[i]

    def ps_done(self, i, ev):
        self.ps_ev[i] = ev

    def ps_reserve(self):
        i = self.ps_free.pop()
        if self.ps_ev[i] is not None:
            self.wait("T", self.ps_ev[i])
            self.ps_ev[i] = None
        return i, self.psum[i]

    def ps_release(self, i, ev=None):
        self.ps_free.append(i)
        self.ps_ev[i] = ev

    def hv(self, off, w, p0=0, p=128):
        return self.heap[p0:p0 + p, off:off + w]

    def hvb(self, offb, w, p0=0, p=128):
        return self.heapb[p0:p0 + p, offb:offb + w]

    def row(self, i, t0=0, w=S):
        return self.small[0:1, i * 2048 + t0:i * 2048 + t0 + w]

    def scal(self, c, w=1, p0=0, p=128):
        return self.small[p0:p0 + p, CSC + c:CSC + c + w]


def bcast_row(b, row_ap, out_ap, width=S, rows=1):
    last = None
    for t0 in range(0, width, TT):
        w = min(TT, width - t0)
        i, ps = b.ps_get()
        mm = b.T.matmul(ps[:, :w], b.s_ones[0:rows, :], row_ap[:, t0:t0 + w],
                        start=True, stop=True)
        e = b.inc("T", mm)
        b.wait("V", e)
        cp = b.V.tensor_copy(out_ap[:, t0:t0 + w], ps[:, :w])
        last = b.inc("V", cp)
        b.ps_done(i, last)
    return last


def part_scalar(b, src11, dst_col):
    i, ps = b.ps_get()
    mm = b.T.matmul(ps[0:128, 0:1], b.s_ones[0:1, :], src11, start=True, stop=True)
    e = b.inc("T", mm)
    b.wait("V", e)
    ev = b.inc("V", b.V.tensor_copy(dst_col, ps[0:128, 0:1]))
    b.ps_done(i, ev)
    return ev


def ladder_round_clip(b, src_ap, dst_ap, P, width, lo, hi, maxlev, zoff, toff,
                      scratch=None):
    t_t = scratch if scratch is not None else b.heap
    inst = None
    for s0 in range(0, width, 2048):
        w = min(2048, width - s0)
        src = src_ap[:, s0:s0 + w]
        dst = dst_ap[:, s0:s0 + w]
        z = t_t[0:P, zoff:zoff + w]
        t = t_t[0:P, toff:toff + w]
        off = float(maxlev)
        b.V.tensor_scalar(z, src, 0.5 + off, None, OP.add)
        L = maxlev
        first = True
        while L >= 1:
            if first:
                b.V.tensor_scalar(dst, z, float(L), float(L), OP.is_ge, OP.mult)
                first = False
            else:
                b.V.scalar_tensor_tensor(t, dst, float(L), z, OP.add, OP.is_le)
                b.V.scalar_tensor_tensor(dst, t, float(L), dst, OP.mult, OP.add)
            L //= 2
        b.V.tensor_scalar(dst, dst, -off, float(lo), OP.add, OP.max)
        inst = b.V.tensor_scalar(dst, dst, float(hi), None, OP.min)
    return b.inc("V", inst)


def pe_transpose(b, src_ap, dst_ap, rows, cols, dtype=F32):
    i, ps = b.ps_get()
    pst = ps if dtype == F32 else ps.bitcast(BF16)
    ident = b.s_id if dtype == F32 else b.s_idb
    mm = b.T.transpose(pst[0:cols, 0:rows], src_ap, ident[0:rows, 0:rows])
    ev = b.inc("T", mm)
    b.wait("V", ev)
    cp = b.V.tensor_copy(dst_ap, pst[0:cols, 0:rows])
    ev2 = b.inc("V", cp)
    b.ps_done(i, ev2)
    return ev2


def proj(b, slabs, KO, fo_list, rhs_fn, evac, dtype=F32, npass=1, rhs_fn2=None,
         ntt=NTT, tt_w=TT, bufs=None):
    if bufs is None:
        bufs = (b.wk if dtype == F32 else b.wkb)
    use_ev = [None, None]
    for idx, fo in enumerate(fo_list):
        bi = idx % 2
        if use_ev[bi] is not None:
            b.wait("DY", use_ev[bi])
        ld = b.dma(bufs[bi][:, 0:KO * 128], slabs[fo])
        b.wait("T", ld)
        for tt in range(ntt):
            t0 = tt * tt_w
            i, ps = b.ps_get()
            for p in range(npass):
                rf = rhs_fn if p == 0 else rhs_fn2
                for ko in range(KO):
                    mm = b.T.matmul(ps[:, :tt_w], bufs[bi][:, ko * 128:(ko + 1) * 128],
                                    rf(ko, t0, tt_w),
                                    start=(p == 0 and ko == 0),
                                    stop=(p == npass - 1 and ko == KO - 1))
            ev_mm = b.inc("T", mm)
            ev_free = evac(idx, fo, tt, ps, ev_mm)
            b.ps_done(i, ev_free)
        use_ev[bi] = b.tick("T")


def evac_hbm(b, dst, act=None, bias_col=None, scale_tiles=None, tt_w=TT):
    def f(idx, fo, tt, ps, ev_mm):
        bt = b.exp_t[tt % 2]
        if b.exp_ev[tt % 2] is not None:
            b.wait("S", b.exp_ev[tt % 2])
            b.wait("V", b.exp_ev[tt % 2])
        src = ps[:, :tt_w]
        if scale_tiles is not None:
            b.wait("V", ev_mm)
            ev = b.inc("V", b.V.tensor_tensor(bt[:, :tt_w], src, scale_tiles(tt), OP.mult))
            src = bt[:, :tt_w]
            ev_mm = ev
        if act is not None:
            b.wait("S", ev_mm)
            kw = {}
            if bias_col is not None:
                kw["bias"] = bias_col(fo)
            ev = b.inc("S", b.SC.activation(bt[:, :tt_w], src, act, **kw))
        elif src is not bt[:, :tt_w]:
            b.wait("V", ev_mm)
            ev = b.inc("V", b.V.tensor_copy(bt[:, :tt_w], src))
        else:
            ev = ev_mm
        b.wait("DY", ev)
        de = b.dma(dst[fo * 128:(fo + 1) * 128, tt * tt_w:(tt + 1) * tt_w], bt[:, :tt_w])
        b.exp_ev[tt % 2] = de
        return ev
    return f


def row_finish(b, banks, n, eps, mean_sub, mrow, rrow):
    b.wait("V", b.ev("T"))
    inv = 1.0 / float(n)
    for tt in range(NTT):
        i0, p0 = banks[2 * tt]
        i1, p1 = banks[2 * tt + 1]
        t0 = tt * TT
        if mean_sub:
            b.V.tensor_scalar(b.row(mrow, t0, TT), p0[0:1, :TT], inv, None, OP.mult)
        ev = b.inc("V", b.V.tensor_scalar(b.row(R_B, t0, TT), p1[0:1, :TT],
                                          inv, None, OP.mult))
        b.ps_release(i0, ev)
        b.ps_release(i1, ev)
    if mean_sub:
        b.V.tensor_tensor(b.row(R_C), b.row(mrow), b.row(mrow), OP.mult)
        b.V.tensor_tensor(b.row(R_B), b.row(R_B), b.row(R_C), OP.subtract)
    ev = b.inc("V", b.V.tensor_scalar(b.row(R_B), b.row(R_B), eps, None, OP.add))
    b.wait("S", ev)
    ev = b.inc("S", b.SC.activation(b.row(R_C), b.row(R_B), AF.Sqrt))
    b.wait("V", ev)
    ev = b.inc("V", b.V.reciprocal(b.row(rrow), b.row(R_C)))
    b.wait("T", ev)
    return ev


def ln_stats(b, xsrc, nfeat, nchunks, eps, mean_sub, mrow, rrow):
    banks = [b.ps_reserve() for _ in range(8)]
    xt = b.hv(WK0, S)
    sq = b.hv(WK1, S)
    ev_prev = None
    for c in range(nchunks):
        e = b.dma(xt, xsrc[c * 128:(c + 1) * 128, :])
        b.wait("S", e)
        if ev_prev is not None:
            b.wait("S", ev_prev)
        ev_s = b.inc("S", b.SC.activation(sq, xt, AF.Square))
        b.wait("T", e)
        b.wait("T", ev_s)
        for tt in range(NTT):
            _, p0 = banks[2 * tt]
            _, p1 = banks[2 * tt + 1]
            b.T.matmul(p0[0:1, :TT], b.s_ones[:, 0:1], xt[:, tt * TT:(tt + 1) * TT],
                       start=(c == 0), stop=(c == nchunks - 1))
            mm = b.T.matmul(p1[0:1, :TT], b.s_ones[:, 0:1], sq[:, tt * TT:(tt + 1) * TT],
                            start=(c == 0), stop=(c == nchunks - 1))
        ev_prev = b.inc("T", mm)
        b.wait("DY", ev_prev)
    return row_finish(b, banks, nfeat, eps, mean_sub, mrow, rrow)


def sq_row_stats(b, chunk_aps, nfeat, eps, rrow):
    banks = [b.ps_reserve() for _ in range(NTT)]
    sq = b.hv(WK1, S)
    n = len(chunk_aps)
    ev_prev = None
    for c, a in enumerate(chunk_aps):
        if ev_prev is not None:
            b.wait("S", ev_prev)
        ev = b.inc("S", b.SC.activation(sq, a, AF.Square))
        b.wait("T", ev)
        for tt in range(NTT):
            _, p = banks[tt]
            mm = b.T.matmul(p[0:1, :TT], b.s_ones[:, 0:1], sq[:, tt * TT:(tt + 1) * TT],
                            start=(c == 0), stop=(c == n - 1))
        ev_prev = b.inc("T", mm)
    b.wait("V", b.ev("T"))
    for tt in range(NTT):
        i, p = banks[tt]
        ev = b.inc("V", b.V.tensor_scalar(b.row(R_B, tt * TT, TT), p[0:1, :TT],
                                          1.0 / float(nfeat), eps, OP.mult, OP.add))
        b.ps_release(i, ev)
    b.wait("S", b.ev("V"))
    ev = b.inc("S", b.SC.activation(b.row(R_C), b.row(R_B), AF.Sqrt))
    b.wait("V", ev)
    ev = b.inc("V", b.V.reciprocal(b.row(rrow), b.row(R_C)))
    b.wait("T", ev)
    return ev


def build_xn(b, xsrc, mrow, rrow, dst_fn, nchunks=NDC, split=None, t0=0, w=S,
             mb_off=WK0, rb_off=WK1, xt_off=GEN):
    mb = b.hv(mb_off, w)
    rb = b.hv(rb_off, w)
    bcast_row(b, b.row(mrow, t0, w), mb, width=w)
    bcast_row(b, b.row(rrow, t0, w), rb, width=w)
    b.barrier()
    xt = b.hv(xt_off, w)
    last = None
    for c in range(nchunks):
        e = b.dma(xt, xsrc[c * 128:(c + 1) * 128, t0:t0 + w])
        b.wait("V", e)
        d = dst_fn(c)
        b.V.tensor_tensor(d, xt, mb, OP.subtract)
        i2 = b.V.tensor_tensor(d, d, rb, OP.mult)
        if split is not None:
            hi, lo = split[0](c), split[1](c)
            b.V.tensor_copy(hi, d)
            i2 = b.V.tensor_tensor(lo, d, hi, OP.subtract)
        last = b.inc("V", i2)
        b.wait("DY", last)
    return last


def attn_softmax(b, nheads, kparts_fn, qparts_fn, vt_fn, vd, scale_ap, abuf_row_fn,
                 prep_head=None):
    bt = b.hv(GEN, TT)
    bt_ev = [None]
    for h in range(nheads):
        if prep_head is not None:
            prep_head(h)
        kparts = kparts_fn(h)
        qparts = qparts_fn(h)
        vt = vt_fn(h)
        for qt in range(NTT):
            t0 = qt * TT
            nkc = (qt + 1) * (TT // 128)
            io, pso = b.ps_reserve()
            idn, psd = b.ps_reserve()
            for kc in range(nkc):
                isc, pss = b.ps_get()
                npart = len(kparts)
                for pi in range(npart):
                    kap, dh = kparts[pi]
                    qap, _ = qparts[pi]
                    mm = b.T.matmul(pss[:, :TT], kap[0:dh, kc * 128:(kc + 1) * 128],
                                    qap[0:dh, t0:t0 + TT],
                                    start=(pi == 0), stop=(pi == npart - 1))
                ev_s = b.inc("T", mm)
                b.wait("V", ev_s)
                ev_m = ev_s
                for qb in range(4):
                    qg = qt * 4 + qb
                    if kc == qg:
                        mi = b.V.tensor_tensor(pss[:, qb * 128:(qb + 1) * 128],
                                               pss[:, qb * 128:(qb + 1) * 128],
                                               b.s_mask[:, :], OP.add)
                        ev_m = b.inc("V", mi)
                    elif kc > qg:
                        mi = b.V.tensor_scalar(pss[:, qb * 128:(qb + 1) * 128],
                                               pss[:, qb * 128:(qb + 1) * 128],
                                               -1.0e30, None, OP.add)
                        ev_m = b.inc("V", mi)
                et = b.exp_t[kc % 2]
                b.wait("S", ev_m)
                if b.exp_ev[kc % 2] is not None:
                    b.wait("S", b.exp_ev[kc % 2])
                ei = b.SC.activation(et[:, :TT], pss[:, :TT], AF.Exp,
                                     bias=b.scal(432), scale=scale_ap)
                ev_e = b.inc("S", ei)
                b.ps_done(isc, ev_e)
                b.wait("T", ev_e)
                b.T.matmul(psd[0:1, :TT], b.s_ones[:, 0:1], et[:, :TT],
                           start=(kc == 0), stop=(kc == nkc - 1))
                mmv = b.T.matmul(pso[0:vd, :TT], vt[:, kc * vd:(kc + 1) * vd],
                                 et[:, :TT], start=(kc == 0), stop=(kc == nkc - 1))
                b.exp_ev[kc % 2] = b.inc("T", mmv)
            b.wait("V", b.ev("T"))
            rr = b.row(R_C, t0, TT)
            ev_r = b.inc("V", b.V.reciprocal(rr, psd[0:1, :TT]))
            b.ps_release(idn, ev_r)
            b.wait("T", ev_r)
            ib, psb = b.ps_get()
            mm = b.T.matmul(psb[0:vd, :TT], b.s_ones[0:1, 0:vd], rr, start=True, stop=True)
            ev_b = b.inc("T", mm)
            b.wait("V", ev_b)
            if bt_ev[0] is not None:
                b.wait("V", bt_ev[0])
            b.V.tensor_copy(bt[0:vd, :TT], psb[0:vd, :TT])
            mi = b.V.tensor_tensor(bt[0:vd, :TT], bt[0:vd, :TT], pso[0:vd, :TT], OP.mult)
            ev_n = b.inc("V", mi)
            b.ps_done(ib, ev_n)
            b.ps_release(io, ev_n)
            b.wait("DY", ev_n)
            bt_ev[0] = b.dma(abuf_row_fn(h)[:, t0:t0 + TT], bt[0:vd, :TT])


def rope_apply(b, io_ap, dh, P_sb, cos_ap, sin_ap, width=S):
    for t0 in range(0, width, TT):
        w = min(TT, width - t0)
        i, ps = b.ps_get()
        mm = b.T.matmul(ps[0:dh, :w], P_sb[0:dh, 0:dh], io_ap[0:dh, t0:t0 + w],
                        start=True, stop=True)
        ev = b.inc("T", mm)
        b.wait("V", ev)
        b.V.tensor_tensor(io_ap[0:dh, t0:t0 + w], io_ap[0:dh, t0:t0 + w],
                          cos_ap[0:dh, t0:t0 + w], OP.mult)
        b.V.tensor_tensor(ps[0:dh, :w], ps[0:dh, :w], sin_ap[0:dh, t0:t0 + w], OP.mult)
        ev2 = b.inc("V", b.V.tensor_tensor(io_ap[0:dh, t0:t0 + w],
                                           io_ap[0:dh, t0:t0 + w], ps[0:dh, :w], OP.add))
        b.ps_done(i, ev2)
    return b.ev("V")


def combine(b, xin, xout):
    xt, st, gt = b.hv(WK0, S), b.hv(WK1, S), b.hv(GEN, S)
    for c in range(NDC):
        b.dma(xt, xin[c * 128:(c + 1) * 128, :])
        b.dma(st, b.sdram[c * 128:(c + 1) * 128, :])
        b.dma(gt, b.gdram[c * 128:(c + 1) * 128, :])
        b.dwait("V")
        b.V.tensor_tensor(st, st, gt, OP.mult)
        ev = b.inc("V", b.V.tensor_tensor(gt, xt, st, OP.add))
        b.wait("DY", ev)
        e4 = b.dma(xout[c * 128:(c + 1) * 128, :], gt)
        b.wait("DY", e4)
    b.barrier()


def hbm_to_arena(b, src, nchunks):
    for c in range(nchunks):
        b.dma(b.hv(A0 + c * S, S), src[c * 128:(c + 1) * 128, :])
    for k in "TVS":
        b.dwait(k)
    b.barrier()


def arena_rhs(b):
    def f(ko, t0, w):
        return b.heap[:, A0 + ko * S + t0: A0 + ko * S + t0 + w]
    return f


EMITTERS = {}


def sq_stats_hbm(b, src, nrows, eps, rrow):
    banks = [b.ps_reserve() for _ in range(NTT)]
    xt = b.hv(WK0, S)
    sq = b.hv(WK1, S)
    n = nrows // 128
    ev_prev = None
    for c in range(n):
        e = b.dma(xt, src[c * 128:(c + 1) * 128, :])
        b.wait("S", e)
        if ev_prev is not None:
            b.wait("S", ev_prev)
        ev = b.inc("S", b.SC.activation(sq, xt, AF.Square))
        b.wait("T", ev)
        for tt in range(NTT):
            _, p = banks[tt]
            mm = b.T.matmul(p[0:1, :TT], b.s_ones[:, 0:1], sq[:, tt * TT:(tt + 1) * TT],
                            start=(c == 0), stop=(c == n - 1))
        ev_prev = b.inc("T", mm)
        b.wait("DY", ev_prev)
    b.wait("V", b.ev("T"))
    for tt in range(NTT):
        i, p = banks[tt]
        ev = b.inc("V", b.V.tensor_scalar(b.row(R_B, tt * TT, TT), p[0:1, :TT],
                                          1.0 / float(nrows), eps, OP.mult, OP.add))
        b.ps_release(i, ev)
    b.wait("S", b.ev("V"))
    ev = b.inc("S", b.SC.activation(b.row(R_C), b.row(R_B), AF.Sqrt))
    b.wait("V", ev)
    ev = b.inc("V", b.V.reciprocal(b.row(rrow), b.row(R_C)))
    b.wait("T", ev)
    return ev


def emit_l0(b, W, xin, xout):
    SCALE = float(192) ** -0.5
    w_qa = b.inp("l0_qa", W["l0_qa"])
    w_qb = b.inp("l0_qb", W["l0_qb"])
    w_kv = b.inp("l0_kv", W["l0_kv"])
    w_kb = b.inp("l0_kb", W["l0_kb"])
    w_o = b.inp("l0_o", W["l0_o"])
    w1 = b.inp("l0_w1", W["l0_w1"])
    w2 = b.inp("l0_w2", W["l0_w2"])
    w3 = b.inp("l0_w3", W["l0_w3"])
    g_qa = b.inp("l0_gqa", pack_gamma(W["l0_qa_ln"]))
    g_kv = b.inp("l0_gkv", pack_gamma(W["l0_kv_ln"]))
    rc, rs, rp = W["l0_rope"]
    ropec = b.inp("l0_ropec", rc)
    ropes = b.inp("l0_ropes", rs)
    ropeP = b.inp("l0_ropeP", rp)

    sb_gqa = b.scal(256, 12)
    sb_gkv = b.scal(268, 4)
    sb_P = b.scal(272, 64, 0, 64)
    b.dma(sb_gqa, g_qa[:, :])
    b.dma(sb_gkv, g_kv[:, :])
    b.dma(sb_P, ropeP[:, :])
    b.dma(b.rope[0:64, 0:S], ropec[:, :])
    e = b.dma(b.rope[0:64, S:2 * S], ropes[:, :])
    for k in "TVS":
        b.wait(k, e)

    ln_stats(b, xin, D, NDC, 1.0e-5, True, R_LNM, R_LNR)
    b.barrier()
    build_xn(b, xin, R_LNM, R_LNR, lambda c: b.hv(A0 + c * S, S))
    b.barrier()
    rhs = arena_rhs(b)

    def kvdst(fo):
        return b.kdram[fo * 128:fo * 128 + 128, :] if fo < 4 else b.kdram[512:640, :]

    proj(b, w_kv, NDC, range(5), rhs, evac_hbm(b, b.kdram))
    b.barrier()
    sq_stats_hbm(b, b.kdram, 512, 1.0e-6, R_B)
    rb2 = b.hv(GEN, S)
    bcast_row(b, b.row(R_B), rb2)
    b.barrier()
    ct = b.hv(WK0, S)
    for c in range(4):
        e = b.dma(ct, b.kdram[c * 128:(c + 1) * 128, :])
        b.wait("V", e)
        b.V.tensor_tensor(ct, ct, rb2, OP.mult)
        ev = b.inc("V", b.V.tensor_scalar(ct, ct, sb_gkv[:, c:c + 1], None, OP.mult))
        b.wait("DY", ev)
        e2 = b.dma(b.kdram[c * 128:(c + 1) * 128, :], ct)
        b.wait("DY", e2)
    kpe_t = b.heap[0:64, WK0:WK0 + S]
    e = b.dma(kpe_t, b.kdram[512:576, :])
    for k in "TV":
        b.wait(k, e)
    rope_apply(b, kpe_t, 64, sb_P, b.rope[0:64, 0:S], b.rope[0:64, S:2 * S])
    ev = b.ev("V")
    b.wait("DY", ev)
    e = b.dma(b.kdram[512:576, :], kpe_t)
    b.wait("DY", e)
    b.barrier()

    proj(b, w_qa, NDC, range(12), rhs, evac_hbm(b, b.qbuf[3072:4608, :]))
    b.barrier()
    sq_stats_hbm(b, b.qbuf[3072:4608, :], 1536, 1.0e-6, R_B)
    rmsb = b.hv(A0 + 12 * S, S)
    bcast_row(b, b.row(R_B), rmsb)
    b.barrier()
    for c in range(12):
        dst = b.hv(A0 + c * S, S)
        e = b.dma(dst, b.qbuf[3072 + c * 128:3072 + (c + 1) * 128, :])
        b.wait("V", e)
        ev = b.inc("V", b.V.tensor_scalar(dst, dst, sb_gqa[:, c:c + 1], None, OP.mult))
        b.wait("DY", ev)
    b.barrier()
    proj(b, w_qb, 12, range(24), rhs,
         evac_hbm(b, b.qbuf[0:3072, :],
                  scale_tiles=lambda tt: rmsb[:, tt * TT:(tt + 1) * TT]))
    b.barrier()

    kh = b.hv(A0, S)
    vh = b.hv(A0 + S, S)
    vt = b.hv(A0 + 2 * S, S)
    qn = b.hv(A0 + 3 * S, S)
    kpe = b.heap[0:64, A0 + 4 * S:A0 + 5 * S]
    qpe = b.heap[0:64, A0 + 5 * S:A0 + 6 * S]
    for c in range(4):
        b.dma(b.hv(A0 + (6 + c) * S, S), b.kdram[c * 128:(c + 1) * 128, :])
    b.dma(kpe, b.kdram[512:576, :])
    for k in "TV":
        b.dwait(k)
    b.barrier()

    def prep_head(h):
        def evac_k(idx, fo, tt, ps, ev_mm):
            dst = kh if idx == 0 else vh
            b.wait("V", ev_mm)
            return b.inc("V", b.V.tensor_copy(dst[:, tt * TT:(tt + 1) * TT], ps[:, :TT]))

        proj(b, w_kb, 4, [2 * h, 2 * h + 1],
             lambda ko, t0, w: b.hv(A0 + (6 + ko) * S + t0, w), evac_k)
        b.barrier()
        for kc in range(16):
            pe_transpose(b, vh[:, kc * 128:(kc + 1) * 128],
                         vt[:, kc * 128:(kc + 1) * 128], 128, 128)
        b.dma(qn, b.qbuf[h * 192:h * 192 + 128, :])
        b.dma(qpe, b.qbuf[h * 192 + 128:h * 192 + 192, :])
        for k in "TV":
            b.dwait(k)
        rope_apply(b, qpe, 64, sb_P, b.rope[0:64, 0:S], b.rope[0:64, S:2 * S])
        b.barrier()

    attn_softmax(b, 16,
                 kparts_fn=lambda h: [(kh, 128), (kpe, 64)],
                 qparts_fn=lambda h: [(qn, 128), (qpe, 64)],
                 vt_fn=lambda h: vt, vd=128, scale_ap=b.scal(433),
                 abuf_row_fn=lambda h: b.abuf[h * 128:(h + 1) * 128, :],
                 prep_head=prep_head)
    b.barrier()

    hbm_to_arena(b, b.abuf, NDC)
    proj(b, w_o, NDC, range(NDC), rhs, evac_hbm(b, b.sdram))
    b.barrier()

    nf4_dequant(b, w1, 44, 16, b.w0q1, 2048)
    nf4_dequant(b, w2, 44, 16, b.w0q2, 2048)
    nf4_dequant(b, w3, 16, 44, b.w0q3, 5632)
    b.barrier()
    l0_ffn(b, xin)
    b.barrier()
    combine(b, xin, xout)


def nf4_dequant(b, wsrc, NCH, KOT, dst_slabs, s1w):
    nblk = s1w // 64
    tot = NCH * 128 * nblk
    ngrp = tot // 256
    wt = b.hv(A0, s1w)
    vv = b.hv(A0 + 5632, s1w)
    uu = b.hv(A0 + 11264, s1w)
    s1 = b.hv(A0 + 16896, NCH * nblk)
    rep = b.hv(A0 + 18304, nblk)
    s1r = b.hv(A0 + 18432, NCH * nblk)
    red1 = b.hv(A0 + 19900, 1)
    bh = b.hvb(40000, s1w)
    mean_col = b.scal(352)
    for c in range(NCH):
        e = b.dma(b.wkb[0][:, 0:s1w], wsrc[c * 128:(c + 1) * 128, :])
        b.wait("V", e)
        b.V.tensor_copy(wt, b.wkb[0][:, 0:s1w])
        red = b.V.reduce_max(s1[:, c * nblk:(c + 1) * nblk],
                             wt.rearrange("p (a c) -> p a c", c=64),
                             AX.X, apply_absolute_value=True)
        ev = b.inc("V", red)
        b.wait("DY", ev)
    ev = b.inc("V", b.V.reduce_sum(red1, s1, AX.X))
    b.wait("T", ev)
    i0, p0 = b.ps_get()
    mm = b.T.matmul(p0[0:1, 0:1], b.s_ones[:, 0:1], red1, start=True, stop=True)
    ev = b.inc("T", mm)
    b.wait("V", ev)
    ev = b.inc("V", b.V.tensor_scalar(b.row(R_B, 0, 1), p0[0:1, 0:1],
                                      1.0 / float(tot), None, OP.mult))
    b.ps_done(i0, ev)
    b.wait("T", ev)
    part_scalar(b, b.row(R_B, 0, 1), mean_col)
    b.barrier()
    b.dma(b.s1flat[0:tot].rearrange("(c p j) -> p c j", p=128, j=nblk),
          s1.rearrange("p (c j) -> p c j", j=nblk), q="DG")
    b.barrier()
    gc = 0
    while gc < ngrp:
        P = min(128, ngrp - gc)
        gt = b.rope[0:P, 0:256]
        cen = b.rope[0:P, 256:512]
        sc = b.rope[0:P, 512:520]
        e = b.dma(gt, b.s1flat[0:tot].rearrange("(g j) -> g j", j=256)[gc:gc + P, :])
        b.wait("V", e)
        b.V.tensor_scalar(cen, gt, mean_col[0:P, :], None, OP.subtract)
        b.V.reduce_max(sc[:, 0:1], cen, AX.X, apply_absolute_value=True)
        b.V.reciprocal(sc[:, 1:2], sc[:, 0:1])
        b.V.tensor_scalar(sc[:, 1:2], sc[:, 1:2], 128.0, None, OP.mult)
        ev = b.inc("V", b.V.tensor_scalar(cen, cen, sc[0:P, 1:2], None, OP.mult))
        ladder_round_clip(b, cen, gt, P, 256, -128.0, 127.0, 256,
                          zoff=1024, toff=1536, scratch=b.rope)
        b.V.tensor_scalar(sc[:, 2:3], sc[:, 0:1], 1.0 / 128.0, None, OP.mult)
        b.V.tensor_scalar(gt, gt, sc[0:P, 2:3], None, OP.mult)
        ev = b.inc("V", b.V.tensor_scalar(gt, gt, mean_col[0:P, :], None, OP.add))
        b.wait("DG", ev)
        b.dma(b.deqflat[0:tot].rearrange("(g j) -> g j", j=256)[gc:gc + P, :], gt, q="DG")
        b.barrier()
        gc += P
    ev = b.inc("V", b.V.reciprocal(s1r, s1))
    stage = b.wkb[1]
    for c in range(NCH):
        e = b.dma(b.wkb[0][:, 0:s1w], wsrc[c * 128:(c + 1) * 128, :])
        e2 = b.dma(rep, b.deqflat[0:tot].rearrange("(c p j) -> p c j", p=128, j=nblk)[:, c, :])
        b.wait("V", e)
        b.wait("V", e2)
        b.V.tensor_copy(wt, b.wkb[0][:, 0:s1w])
        b.V.tensor_tensor(wt.rearrange("p (a c) -> p a c", c=64),
                          wt.rearrange("p (a c) -> p a c", c=64),
                          s1r[:, c * nblk:(c + 1) * nblk]
                          .unsqueeze(2).broadcast_to([128, nblk, 64]), OP.mult)
        b.V.tensor_scalar(vv, wt, float(NF4_THR[0]), float(NF4_DEL[0]), OP.is_gt, OP.mult)
        for m in range(1, 15):
            b.V.tensor_scalar(uu, wt, float(NF4_THR[m]), float(NF4_DEL[m]), OP.is_gt, OP.mult)
            b.V.tensor_tensor(vv, vv, uu, OP.add)
        b.V.scalar_tensor_tensor(vv.rearrange("p (a c) -> p a c", c=64),
                                 vv.rearrange("p (a c) -> p a c", c=64), float(NF4[0]),
                                 rep.unsqueeze(2).broadcast_to([128, nblk, 64]),
                                 OP.add, OP.mult)
        ev = b.inc("V", b.V.tensor_copy(bh, vv))
        b.wait("T", ev)
        for t in range(KOT):
            pe_transpose(b, bh.rearrange("p (a c) -> p a c", c=128)[:, t, :],
                         stage[:, t * 128:(t + 1) * 128], 128, 128, BF16)
        ev = b.ev("V")
        b.wait("DY", ev)
        e3 = b.dma(dst_slabs[c], stage[:, 0:KOT * 128])
        b.wait("DG", e3)
        b.wait("DY", e3)
    b.barrier()


def l0_ffn(b, xin):
    XH, XL, HH, HL = 0, 8192, 16384, 38912
    for tt4 in range(NTT):
        t0 = tt4 * TT
        build_xn(b, xin, R_LNM, R_LNR,
                 lambda c: b.hv(A0 + 31 * 1024, TT),
                 split=(lambda c: b.hvb(XH + c * TT, TT),
                        lambda c: b.hvb(XL + c * TT, TT)),
                 t0=t0, w=TT, mb_off=WK0, rb_off=WK1, xt_off=GEN)
        b.barrier()

        def rhs_hi(ko, _t0, w):
            return b.hvb(XH + ko * TT, w)

        def rhs_lo(ko, _t0, w):
            return b.hvb(XL + ko * TT, w)

        t1 = b.hv(GEN, TT)
        h32 = b.hv(GEN + TT, TT)
        hev = [None]
        for fo in range(44):
            ld1 = b.dma(b.wkb[0][:, 0:16 * 128], b.w0q1[fo])
            ld2 = b.dma(b.wkb[1][:, 0:16 * 128], b.w0q2[fo])
            b.wait("T", ld1)
            ia, psA = b.ps_get()
            for p, rf in ((0, rhs_hi), (1, rhs_lo)):
                for ko in range(16):
                    mmA = b.T.matmul(psA[:, :TT], b.wkb[0][:, ko * 128:(ko + 1) * 128],
                                     rf(ko, 0, TT), start=(p == 0 and ko == 0),
                                     stop=(p == 1 and ko == 15))
            evA = b.inc("T", mmA)
            b.wait("T", ld2)
            ib2, psB = b.ps_get()
            for p, rf in ((0, rhs_hi), (1, rhs_lo)):
                for ko in range(16):
                    mmB = b.T.matmul(psB[:, :TT], b.wkb[1][:, ko * 128:(ko + 1) * 128],
                                     rf(ko, 0, TT), start=(p == 0 and ko == 0),
                                     stop=(p == 1 and ko == 15))
            evB = b.inc("T", mmB)
            b.wait("S", evA)
            if hev[0] is not None:
                b.wait("S", hev[0])
            ev = b.inc("S", b.SC.activation(t1, psA[:, :TT], AF.Silu))
            b.ps_done(ia, ev)
            b.wait("V", ev)
            b.wait("V", evB)
            ev = b.inc("V", b.V.tensor_tensor(h32, t1, psB[:, :TT], OP.mult))
            b.ps_done(ib2, ev)
            hh = b.hvb(HH + fo * TT, TT)
            hl = b.hvb(HL + fo * TT, TT)
            b.V.tensor_copy(hh, h32)
            ev = b.inc("V", b.V.tensor_tensor(hl, h32, hh, OP.subtract))
            hev[0] = ev
            te = b.tick("T")
            b.wait("DY", te)
        b.barrier()

        def rhs_hh(ko, _t0, w):
            return b.hvb(HH + ko * TT, w)

        def rhs_hl(ko, _t0, w):
            return b.hvb(HL + ko * TT, w)

        def evac_g(idx, fo, _tt, ps, ev_mm):
            bt = b.exp_t[fo % 2]
            if b.exp_ev[fo % 2] is not None:
                b.wait("S", b.exp_ev[fo % 2])
            b.wait("S", ev_mm)
            ev = b.inc("S", b.SC.activation(bt[:, :TT], ps[:, :TT], AF.Silu))
            b.wait("DY", ev)
            b.exp_ev[fo % 2] = b.dma(b.gdram[fo * 128:(fo + 1) * 128, t0:t0 + TT],
                                     bt[:, :TT])
            return ev

        proj(b, b.w0q3, 44, range(16), rhs_hh, evac_g, dtype=BF16, npass=2,
             rhs_fn2=rhs_hl, ntt=1, tt_w=TT)
        b.barrier()


EMITTERS[0] = emit_l0


def emit_l1(b, W, xin, xout):
    H, CH, NCH = 16, 64, 32
    wq = b.inp("l1_wq", W["l1_wq"])
    wk = b.inp("l1_wk", W["l1_wk"])
    wv = b.inp("l1_wv", W["l1_wv"])
    wo = b.inp("l1_wo", W["l1_wo"])
    w1 = b.inp("l1_w1", W["l1_w1"])
    w2 = b.inp("l1_w2", W["l1_w2"])
    wr = b.inp("l1_wr", pack_gamma(W["l1_wr"].reshape(-1)))
    br = b.inp("l1_br", W["l1_br"].reshape(1, 1).astype(np.float32))
    b1 = b.inp("l1_b1", pack_gamma(W["l1_b1"]))
    b2 = b.inp("l1_b2", pack_gamma(W["l1_b2"]))
    rc, rs, rp = W["l1_rope"]
    ropec = b.inp("l1_ropec", rc)
    ropes = b.inp("l1_ropes", rs)
    ropeP = b.inp("l1_ropeP", rp)
    diagT = b.inp("l1_diagT", W["l1_diagT"])
    qdec = b.inp("l1_qdec", W["l1_qdec"])
    kdec = b.inp("l1_kdec", W["l1_kdec"])

    sb_wr = b.scal(0, 16)
    sb_kdec = b.scal(16, 16, 0, 64)
    sb_b1 = b.scal(32, 64)
    sb_b2 = b.scal(96, 16)
    sb_wT = b.scal(112, 16)
    sb_P = b.scal(128, 128)
    sb_br = b.scal(392, 1, 0, 1)
    b.dma(sb_wr, wr[:, :])
    b.dma(sb_kdec, kdec[:, :])
    b.dma(sb_b1, b1[:, :])
    b.dma(sb_b2, b2[:, :])
    b.dma(sb_P, ropeP[:, :])
    b.dma(sb_br, br[:, :])
    sb_diag = b.heap[0:64, GEN:GEN + 1024]
    sb_qdec = b.hv(GEN + 1024, 1024)
    b.dma(b.rope[:, 0:S], ropec[:, :])
    e = b.dma(b.rope[:, S:2 * S], ropes[:, :])
    for k in "TVS":
        b.wait(k, e)

    ln_stats(b, xin, D, NDC, 1.0e-5, True, R_LNM, R_LNR)
    b.barrier()
    build_xn(b, xin, R_LNM, R_LNR, lambda c: b.hv(A0 + c * S, S))
    b.barrier()
    rhs = arena_rhs(b)

    proj(b, wq, NDC, range(16), rhs, evac_hbm(b, b.qbuf[0:2048, :]))
    proj(b, wk, NDC, range(4), rhs, evac_hbm(b, b.kdram[0:512, :]))
    proj(b, wv, NDC, range(4), rhs, evac_hbm(b, b.vdram[0:512, :]))
    b.barrier()

    for tc in range(16):
        i, ps = b.ps_get()
        for ko in range(NDC):
            mm = b.T.matmul(ps[0:128, 0:1],
                            b.heap[:, A0 + ko * S + tc * 128:A0 + ko * S + (tc + 1) * 128],
                            sb_wr[:, ko:ko + 1], start=(ko == 0), stop=(ko == 15))
        ev = b.inc("T", mm)
        b.wait("V", ev)
        ev = b.inc("V", b.V.tensor_copy(sb_wT[:, tc:tc + 1], ps[0:128, 0:1]))
        b.ps_done(i, ev)
    part_scalar(b, sb_br, b.scal(396))
    b.barrier()
    ev = b.inc("V", b.V.tensor_scalar(sb_wT, sb_wT, b.scal(396), None, OP.add))
    b.barrier()
    pe_transpose(b, sb_wT, b.heap[0:16, WK0:WK0 + 128], 128, 16)
    b.barrier()
    b.dma(b.row(R_B), b.heap[0:16, WK0:WK0 + 128], q="DG")
    b.barrier()
    wb = b.hv(WK1, S)
    bcast_row(b, b.row(R_B), wb)
    b.barrier()
    rb_banks = [b.ps_reserve() for _ in range(NTT)]
    Ct = b.hv(WK0, S)
    ev_prev = None
    for tc in range(16):
        if ev_prev is not None:
            b.wait("V", ev_prev)
        ev = b.inc("V", b.V.tensor_scalar(Ct, wb, sb_wT[:, tc:tc + 1], None, OP.is_lt))
        b.wait("T", ev)
        for tt in range(NTT):
            _, p = rb_banks[tt]
            mm = b.T.matmul(p[0:1, :TT], b.s_ones[:, 0:1], Ct[:, tt * TT:(tt + 1) * TT],
                            start=(tc == 0), stop=(tc == 15))
        ev_prev = b.inc("T", mm)
    b.wait("V", b.ev("T"))
    for tt in range(NTT):
        i, p = rb_banks[tt]
        ev = b.inc("V", b.V.tensor_scalar(b.row(R_C, tt * TT, TT), p[0:1, :TT],
                                          510.5, None, OP.is_lt))
        b.ps_release(i, ev)
    ev = b.inc("V", b.V.tensor_tensor(b.row(R_B), b.row(R_B), b.row(R_C), OP.mult))
    b.barrier()

    proj(b, w1, NDC, range(64), rhs,
         evac_hbm(b, b.hdram, act=AF.Gelu_apprx_tanh,
                  bias_col=lambda fo: sb_b1[:, fo:fo + 1]))
    b.barrier()

    b.dma(sb_diag, diagT[:, :])
    b.dma(sb_qdec, qdec[:, :])
    b.barrier()
    qh = b.hv(A0, S)
    kg = b.hv(A0 + S, S)
    vtg = b.heap[0:64, A0 + 2 * S:A0 + 4 * S]
    ktg = b.heap[0:64, A0 + 4 * S:A0 + 6 * S]
    oh = b.hv(A0 + 6 * S, S)
    vload = b.hv(A0 + 7 * S, S)
    kv = b.hv(A0 + 8 * S, 128)
    st = b.heap[0:64, A0 + 8 * S + 128:A0 + 8 * S + 192]
    qd = b.hv(A0 + 8 * S + 256, 64)
    kd = b.heap[0:64, A0 + 8 * S + 384:A0 + 8 * S + 512]

    def tm(t, cc):
        return t[0:64, cc * 128:(cc + 1) * 128]

    cur_g = [-1]
    for h in range(H):
        g = h // 4
        if cur_g[0] != g:
            cur_g[0] = g
            e = b.dma(kg, b.kdram[g * 128:(g + 1) * 128, :])
            for k in "TV":
                b.wait(k, e)
            rope_apply(b, kg, 128, sb_P, b.rope[:, 0:S], b.rope[:, S:2 * S])
            e = b.dma(vload, b.vdram[g * 128:(g + 1) * 128, :])
            b.wait("T", e)
            b.barrier()
            for cc in range(NCH):
                pe_transpose(b, vload[:, cc * 64:(cc + 1) * 64], tm(vtg, cc), 128, 64)
                pe_transpose(b, kg[:, cc * 64:(cc + 1) * 64], tm(ktg, cc), 128, 64)
            b.barrier()
        e = b.dma(qh, b.qbuf[h * 128:(h + 1) * 128, :])
        for k in "TV":
            b.wait(k, e)
        rope_apply(b, qh, 128, sb_P, b.rope[:, 0:S], b.rope[:, S:2 * S])
        ev = b.inc("V", b.V.memset(kv, 0.0))
        b.barrier()
        lam = 2.0 ** (-(h + 1) / 2.0)
        blk = math.exp(-lam * CH)
        for cc in range(NCH):
            isc, pss = b.ps_get()
            mm = b.T.matmul(pss[0:64, 0:64], kg[:, cc * 64:(cc + 1) * 64],
                            qh[:, cc * 64:(cc + 1) * 64], start=True, stop=True)
            ev_s = b.inc("T", mm)
            b.wait("V", ev_s)
            ev_st = b.inc("V", b.V.tensor_tensor(st, pss[0:64, 0:64],
                                                 sb_diag[:, h * 64:(h + 1) * 64], OP.mult))
            b.ps_done(isc, ev_st)
            ev_qd = b.inc("V", b.V.tensor_tensor(qd[:, 0:64], qh[:, cc * 64:(cc + 1) * 64],
                                                 sb_qdec[:, h * 64:(h + 1) * 64], OP.mult))
            io, pso = b.ps_get()
            b.wait("T", ev_st)
            b.wait("T", ev_qd)
            b.T.matmul(pso[:, 0:64], tm(vtg, cc), st, start=True, stop=False)
            mm = b.T.matmul(pso[:, 0:64], kv, qd[:, 0:64], start=False, stop=True)
            ev_o = b.inc("T", mm)
            b.wait("V", ev_o)
            ev = b.inc("V", b.V.tensor_copy(oh[:, cc * 64:(cc + 1) * 64], pso[:, 0:64]))
            b.ps_done(io, ev)
            ev_kd = b.inc("V", b.V.tensor_scalar(kd, tm(ktg, cc),
                                                 sb_kdec[:, h:h + 1], None, OP.mult))
            ik, psk = b.ps_get()
            b.wait("T", ev_kd)
            mm = b.T.matmul(psk[:, 0:128], kd, tm(vtg, cc), start=True, stop=True)
            ev_k = b.inc("T", mm)
            b.wait("V", ev_k)
            b.V.tensor_scalar(kv, kv, blk, None, OP.mult)
            ev = b.inc("V", b.V.tensor_tensor(kv, kv, psk[:, 0:128], OP.add))
            b.ps_done(ik, ev)
            b.wait("T", ev)
        ev = b.ev("V")
        b.wait("DY", ev)
        b.dma(b.abuf[h * 128:(h + 1) * 128, :], oh)
        b.barrier()

    hbm_to_arena(b, b.abuf, NDC)
    proj(b, wo, NDC, range(NDC), rhs, evac_hbm(b, b.sdram))
    b.barrier()

    TTW = 256
    wkslab = b.hv(A0 + 16384, 8192)
    bt = b.hv(A0 + 16384 + 8192, TTW)
    for t8 in range(S // TTW):
        t0 = t8 * TTW
        for c in range(64):
            b.dma(b.hv(A0 + c * TTW, TTW), b.hdram[c * 128:(c + 1) * 128, t0:t0 + TTW])
        b.dwait("T")
        for fo in range(16):
            ld = b.dma(wkslab, w2[fo])
            b.wait("T", ld)
            i, ps = b.ps_get()
            for ko in range(64):
                mm = b.T.matmul(ps[:, :TTW], wkslab[:, ko * 128:(ko + 1) * 128],
                                b.hv(A0 + ko * TTW, TTW), start=(ko == 0), stop=(ko == 63))
            ev_mm = b.inc("T", mm)
            b.wait("S", ev_mm)
            ev = b.inc("S", b.SC.activation(bt, ps[:, :TTW], AF.Identity,
                                            bias=sb_b2[:, fo:fo + 1]))
            b.ps_done(i, ev)
            b.wait("DY", ev)
            e2 = b.dma(b.gdram[fo * 128:(fo + 1) * 128, t0:t0 + TTW], bt)
            b.wait("T", e2)
        b.barrier()

    mb = b.hv(A0, S)
    rb = b.hv(A0 + S, S)
    gmulb = b.hv(A0 + 2 * S, S)
    selb = b.hv(A0 + 3 * S, S)
    bcast_row(b, b.row(R_LNM), mb)
    bcast_row(b, b.row(R_LNR), rb)
    bcast_row(b, b.row(R_B), gmulb)
    bcast_row(b, b.row(R_C), selb)
    b.barrier()
    xt, bt2, gg = b.hv(A0 + 4 * S, S), b.hv(A0 + 5 * S, S), b.hv(A0 + 6 * S, S)
    for c in range(NDC):
        e1 = b.dma(xt, xin[c * 128:(c + 1) * 128, :])
        e2 = b.dma(bt2, b.gdram[c * 128:(c + 1) * 128, :])
        b.wait("V", e1)
        b.V.tensor_tensor(xt, xt, mb, OP.subtract)
        b.V.tensor_tensor(xt, xt, rb, OP.mult)
        b.wait("V", e2)
        b.V.tensor_tensor(bt2, bt2, gmulb, OP.mult)
        b.V.tensor_tensor(gg, xt, selb, OP.mult)
        b.V.tensor_tensor(gg, xt, gg, OP.subtract)
        ev = b.inc("V", b.V.tensor_tensor(gg, gg, bt2, OP.add))
        b.wait("S", ev)
        ev = b.inc("S", b.SC.activation(gg, gg, AF.Silu))
        b.wait("DY", ev)
        e3 = b.dma(b.gdram[c * 128:(c + 1) * 128, :], gg)
        b.wait("DY", e3)
    b.barrier()
    combine(b, xin, xout)


EMITTERS[1] = emit_l1


def wmean_prepass(b, wf, FO, KO, col, eps=1.0e-5):
    acc = b.scal(400, 1)
    red = b.scal(404, 1)
    ev = b.inc("V", b.V.memset(acc, 0.0))
    for fo in range(FO):
        e = b.dma(b.hv(WK0, KO * 128), wf[fo])
        b.wait("V", e)
        b.V.reduce_sum(red, b.hv(WK0, KO * 128), AX.X, apply_absolute_value=True)
        ev = b.inc("V", b.V.tensor_tensor(acc, acc, red, OP.add))
        b.wait("DY", ev)
    b.wait("T", ev)
    i, ps = b.ps_get()
    mm = b.T.matmul(ps[0:1, 0:1], b.s_ones[:, 0:1], acc, start=True, stop=True)
    ev = b.inc("T", mm)
    b.wait("V", ev)
    b.V.tensor_scalar(b.row(R_B, 0, 1), ps[0:1, 0:1],
                      1.0 / float(FO * KO * 128 * 128), eps, OP.mult, OP.max)
    ev = b.inc("V", b.V.tensor_scalar(b.row(R_B, 1, 1), b.row(R_B, 0, 1),
                                      0.5, None, OP.mult))
    b.ps_done(i, ev)
    b.wait("T", ev)
    part_scalar(b, b.row(R_B, 0, 1), b.scal(col))
    part_scalar(b, b.row(R_B, 1, 1), b.scal(col + 1))
    b.barrier()


def ternarize_prepass(b, wf, FO, KO, dst, thr_col):
    width = KO * 128
    for fo in range(FO):
        for s0 in range(0, width, 2048):
            w = min(2048, width - s0)
            buf = b.hv(WK0, w)
            e = b.dma(buf, wf[fo][:, s0:s0 + w])
            b.wait("V", e)
            t1 = b.rope[:, 0:w]
            t2 = b.rope[:, 2048:2048 + w]
            q = b.hvb(2 * WK1, w)
            b.V.tensor_scalar(t1, buf, b.scal(thr_col), None, OP.is_gt)
            b.V.tensor_scalar(t2, buf, -1.0, None, OP.mult)
            b.V.tensor_scalar(t2, t2, b.scal(thr_col), None, OP.is_gt)
            ev = b.inc("V", b.V.tensor_tensor(q, t1, t2, OP.subtract))
            b.wait("DY", ev)
            b.wait("DG", ev)
            e2 = b.dma(dst[fo][:, s0:s0 + w], q, q="DG")
            b.wait("DY", e2)
            b.wait("V", e2)
    b.barrier()


def tok_quant_tc(b, tc, nfc, src_fn, dst_fn, rec_col, eps_rms, seg_fc=16):
    segs = [(s, min(s + seg_fc, nfc)) for s in range(0, nfc, seg_fc)]
    ssq = b.scal(408, 1)
    mx = b.scal(412, 1)
    s2 = b.scal(416, 1)
    s3 = b.scal(420, 1)
    rstd = b.scal(424, 1)
    scl = b.scal(428, 1)
    first = True
    for (f0, f1) in segs:
        W = (f1 - f0) * 128
        tm = b.rope[:, 0:W]
        for fc in range(f0, f1):
            pe_transpose(b, src_fn(fc), tm[:, (fc - f0) * 128:(fc - f0 + 1) * 128],
                         128, 128)
        b.barrier()
        sq = b.rope[:, 2048:2048 + W]
        ev = b.inc("S", b.SC.activation(sq, tm, AF.Square))
        b.wait("V", ev)
        if first:
            b.V.reduce_sum(ssq, sq, AX.X)
            ev = b.inc("V", b.V.reduce_max(mx, tm, AX.X, apply_absolute_value=True))
            first = False
        else:
            b.V.reduce_sum(s2, sq, AX.X)
            b.V.tensor_tensor(ssq, ssq, s2, OP.add)
            b.V.reduce_max(s3, tm, AX.X, apply_absolute_value=True)
            ev = b.inc("V", b.V.tensor_tensor(mx, mx, s3, OP.max))
        b.barrier()
    b.V.tensor_scalar(ssq, ssq, 1.0 / float(nfc * 128), eps_rms, OP.mult, OP.add)
    ev = b.inc("V", b.V.engine_nop())
    b.wait("S", ev)
    ev = b.inc("S", b.SC.activation(s2, ssq, AF.Sqrt))
    b.wait("V", ev)
    b.V.reciprocal(rstd, s2)
    b.V.tensor_tensor(mx, mx, rstd, OP.mult)
    b.V.tensor_scalar(mx, mx, 1.0e-5, None, OP.max)
    b.V.tensor_scalar(b.scal(rec_col + tc), mx, 1.0 / 127.0, None, OP.mult)
    b.V.reciprocal(scl, mx)
    ev = b.inc("V", b.V.tensor_scalar(scl, scl, 127.0, None, OP.mult))
    b.barrier()
    for (f0, f1) in segs:
        W = (f1 - f0) * 128
        tm = b.rope[:, 0:W]
        for fc in range(f0, f1):
            pe_transpose(b, src_fn(fc), tm[:, (fc - f0) * 128:(fc - f0 + 1) * 128],
                         128, 128)
        b.barrier()
        b.V.tensor_scalar(tm, tm, rstd, scl, OP.mult, OP.mult)
        ints = b.rope[:, 2048:2048 + W]
        ladder_round_clip(b, tm, ints, 128, W, -128.0, 127.0, 128,
                          zoff=WK0, toff=WK1)
        bh = b.hvb(2 * GEN, W)
        ev = b.inc("V", b.V.tensor_copy(bh, ints))
        b.wait("T", ev)
        for fc in range(f0, f1):
            pe_transpose(b, bh.rearrange("p (a c) -> p a c", c=128)[:, fc - f0, :],
                         dst_fn(fc), 128, 128, BF16)
        b.barrier()


def emit_l2(b, W, xin, xout):
    wq = b.inp("l2_wq", W["l2_wq"])
    wk = b.inp("l2_wk", W["l2_wk"])
    wv = b.inp("l2_wv", W["l2_wv"])
    wo = b.inp("l2_wo", W["l2_wo"])
    bo = b.inp("l2_bo", pack_gamma(W["l2_bo"]))
    w1f = b.inp("l2_w1", W["l2_w1"])
    w2f = b.inp("l2_w2", W["l2_w2"])
    w3f = b.inp("l2_w3", W["l2_w3"])
    sb_bo = b.scal(336, 16)
    e = b.dma(sb_bo, bo[:, :])
    b.wait("S", e)

    ln_stats(b, xin, D, NDC, 1.0e-5, True, R_LNM, R_LNR)
    b.barrier()
    build_xn(b, xin, R_LNM, R_LNR, lambda c: b.hv(A0 + c * S, S))
    b.barrier()
    rhs = arena_rhs(b)

    proj(b, wq, NDC, range(16), rhs, evac_hbm(b, b.qbuf[0:2048, :]))
    proj(b, wk, NDC, [0], rhs, evac_hbm(b, b.kdram))
    proj(b, wv, NDC, [0], rhs, evac_hbm(b, b.vdram))
    b.barrier()

    qh = b.hv(A0, S)
    kres = b.hv(A0 + S, S)
    vres = b.hv(A0 + 2 * S, S)
    vt = b.hv(A0 + 3 * S, S)
    b.dma(kres, b.kdram[0:128, :])
    b.dma(vres, b.vdram[0:128, :])
    for k in "TV":
        b.dwait(k)
    for kc in range(16):
        pe_transpose(b, vres[:, kc * 128:(kc + 1) * 128], vt[:, kc * 128:(kc + 1) * 128],
                     128, 128)
    b.barrier()

    def prep_head(h):
        e = b.dma(qh, b.qbuf[h * 128:(h + 1) * 128, :])
        b.wait("T", e)

    attn_softmax(b, 16, kparts_fn=lambda h: [(kres, 128)],
                 qparts_fn=lambda h: [(qh, 128)], vt_fn=lambda h: vt, vd=128,
                 scale_ap=b.scal(434),
                 abuf_row_fn=lambda h: b.abuf[h * 128:(h + 1) * 128, :],
                 prep_head=prep_head)
    b.barrier()

    hbm_to_arena(b, b.abuf, NDC)
    proj(b, wo, NDC, range(NDC), rhs,
         evac_hbm(b, b.sdram, act=AF.Identity, bias_col=lambda fo: sb_bo[:, fo:fo + 1]))
    b.barrier()

    CW1, CW2, CW3 = 352, 354, 356
    wmean_prepass(b, w1f, 44, 16, CW1)
    wmean_prepass(b, w2f, 44, 16, CW2)
    wmean_prepass(b, w3f, 16, 44, CW3)
    ternarize_prepass(b, w1f, 44, 16, b.l2q1, CW1 + 1)
    ternarize_prepass(b, w2f, 44, 16, b.l2q2, CW2 + 1)
    ternarize_prepass(b, w3f, 16, 44, b.l2q3, CW3 + 1)

    build_xn(b, xin, R_LNM, R_LNR, lambda c: b.hv(A0 + c * S, S))
    b.barrier()
    REC1 = 360
    for tc in range(16):
        tok_quant_tc(b, tc, 16,
                     lambda fc: b.hv(A0 + fc * S + tc * 128, 128),
                     lambda fc: b.hvb(2 * (A0 + fc * S) + tc * 256, 128),
                     REC1, 1.0e-6)
    pe_transpose(b, b.scal(REC1, 16), b.heap[0:16, WK0:WK0 + 128], 128, 16)
    b.barrier()
    b.dma(b.row(R_B), b.heap[0:16, WK0:WK0 + 128], q="DG")
    b.barrier()
    recb = b.hv(WK0, S)
    bcast_row(b, b.row(R_B), recb)
    b.barrier()
    r1 = b.hv(WK1, S)
    r2 = b.hv(GEN, S)
    b.V.tensor_scalar(r1, recb, b.scal(CW1), None, OP.mult)
    ev = b.inc("V", b.V.tensor_scalar(r2, recb, b.scal(CW2), None, OP.mult))
    b.barrier()

    def xq1_rhs(ko, t0, w):
        base = 2 * (A0 + ko * S)
        return b.hvb(base, 2 * S).rearrange("p (t j) -> p t j", j=256)[
            :, t0 // 128:t0 // 128 + w // 128, 0:128]

    bufs2 = [b.hvb(2 * WK0, 2048), b.hvb(2 * WK0 + 2048, 2048)]
    ht = b.rope[:, 0:2048]
    g1 = b.rope[:, 2048:2560]
    u1 = b.rope[:, 2560:3072]
    ht_ev = [None]
    for fo in range(44):
        if ht_ev[0] is not None:
            b.wait("V", ht_ev[0])
        ld1 = b.dma(bufs2[0], b.l2q1[fo])
        ld2 = b.dma(bufs2[1], b.l2q2[fo])
        b.wait("T", ld1)
        b.wait("T", ld2)
        for tt in range(NTT):
            t0 = tt * TT
            ia, psA = b.ps_get()
            for ko in range(16):
                mmA = b.T.matmul(psA[:, :TT], bufs2[0][:, ko * 128:(ko + 1) * 128],
                                 xq1_rhs(ko, t0, TT), start=(ko == 0), stop=(ko == 15))
            evA = b.inc("T", mmA)
            ib, psB = b.ps_get()
            for ko in range(16):
                mmB = b.T.matmul(psB[:, :TT], bufs2[1][:, ko * 128:(ko + 1) * 128],
                                 xq1_rhs(ko, t0, TT), start=(ko == 0), stop=(ko == 15))
            evB = b.inc("T", mmB)
            b.wait("V", evA)
            ev = b.inc("V", b.V.tensor_tensor(g1, psA[:, :TT], r1[:, t0:t0 + TT], OP.mult))
            b.ps_done(ia, ev)
            b.wait("S", ev)
            ev = b.inc("S", b.SC.activation(g1, g1, AF.Silu))
            b.wait("V", ev)
            b.wait("V", evB)
            ev = b.inc("V", b.V.tensor_tensor(u1, psB[:, :TT], r2[:, t0:t0 + TT], OP.mult))
            b.ps_done(ib, ev)
            ev = b.inc("V", b.V.tensor_tensor(ht[:, t0:t0 + TT], g1, u1, OP.mult))
        ev = b.ev("V")
        b.wait("DY", ev)
        ht_ev[0] = b.dma(b.hdram[fo * 128:(fo + 1) * 128, :], ht)
        te = b.tick("T")
        b.wait("DY", te)
    b.barrier()

    REC2 = 376
    for tt in range(NTT):
        t0 = tt * TT
        for tq in range(4):
            tc = tt * 4 + tq
            hl = b.hv(A0 + 24576, 5632)
            for fc in range(44):
                b.dma(hl[:, fc * 128:(fc + 1) * 128],
                      b.hdram[fc * 128:(fc + 1) * 128, tc * 128:(tc + 1) * 128])
            for k in "TV":
                b.dwait(k)
            tok_quant_tc(b, tc, 44,
                         lambda fc: hl[:, fc * 128:(fc + 1) * 128],
                         lambda fc, _tq=tq: b.hvb(2 * A0 + fc * 2 * TT + _tq * 256, 128),
                         REC2 - tt * 4, 1.0e-6, seg_fc=16)
        pe_transpose(b, b.scal(REC2, 4), b.heap[0:4, WK0:WK0 + 128], 128, 4)
        b.barrier()
        b.dma(b.row(R_B, t0, TT), b.heap[0:4, WK0:WK0 + 128], q="DG")
        b.barrier()
        r3 = b.hv(GEN + 1536, TT)
        bcast_row(b, b.row(R_B, t0, TT), r3, width=TT)
        b.barrier()
        ev = b.inc("V", b.V.tensor_scalar(r3, r3, b.scal(CW3), None, OP.mult))
        b.barrier()

        def xq2_rhs(ko, _t0, w):
            return b.hvb(2 * A0 + ko * 2 * TT, 2 * TT).rearrange(
                "p (t j) -> p t j", j=256)[:, 0:w // 128, 0:128]

        def evac_g(idx, fo, _tt, ps, ev_mm):
            bt = b.exp_t[fo % 2]
            if b.exp_ev[fo % 2] is not None:
                b.wait("V", b.exp_ev[fo % 2])
            b.wait("V", ev_mm)
            ev = b.inc("V", b.V.tensor_tensor(bt[:, :TT], ps[:, :TT], r3, OP.mult))
            b.wait("S", ev)
            ev = b.inc("S", b.SC.activation(bt[:, :TT], bt[:, :TT], AF.Silu))
            b.wait("DY", ev)
            b.exp_ev[fo % 2] = b.dma(b.gdram[fo * 128:(fo + 1) * 128, t0:t0 + TT],
                                     bt[:, :TT])
            return ev

        bufs3 = [b.hvb(2 * WK0, 5632), b.hvb(2 * WK0 + 5632, 5632)]
        proj(b, b.l2q3, 44, range(16), xq2_rhs, evac_g, dtype=BF16, ntt=1, tt_w=TT,
             bufs=bufs3)
        b.barrier()
    combine(b, xin, xout)


EMITTERS[2] = emit_l2


def ternary_dequant(b, wsrc, NCH, width, KOT, dst_slabs):
    nblk = width // 64
    for c in range(NCH):
        wt = b.hv(A0, width)
        e = b.dma(wt, wsrc[c * 128:(c + 1) * 128, :])
        b.wait("V", e)
        sc = b.hv(A0 + 8192, nblk)
        b.V.reduce_sum(sc, wt.rearrange("p (a c) -> p a c", c=64), AX.X,
                       apply_absolute_value=True)
        b.V.tensor_scalar(sc, sc, 1.0 / 64.0, 1.0e-8, OP.mult, OP.max)
        thr = b.hv(A0 + 8192 + 128, nblk)
        b.V.tensor_scalar(thr, sc, 0.5, None, OP.mult)
        thr3 = thr.unsqueeze(2).broadcast_to([128, nblk, 64])
        wt3 = wt.rearrange("p (a c) -> p a c", c=64)
        t1 = b.hv(A0 + 8704, width)
        t2 = b.hv(A0 + 8704 + width, width)
        b.V.tensor_tensor(t1.rearrange("p (a c) -> p a c", c=64), wt3, thr3, OP.is_gt)
        b.V.tensor_scalar(t2, wt, -1.0, None, OP.mult)
        b.V.tensor_tensor(t2.rearrange("p (a c) -> p a c", c=64),
                          t2.rearrange("p (a c) -> p a c", c=64), thr3, OP.is_gt)
        b.V.tensor_tensor(t1, t1, t2, OP.subtract)
        bh = b.hvb(2 * (A0 + 8704 + 2 * width), width)
        ev = b.inc("V", b.V.tensor_tensor(
            bh.rearrange("p (a c) -> p a c", c=64),
            t1.rearrange("p (a c) -> p a c", c=64),
            sc.unsqueeze(2).broadcast_to([128, nblk, 64]), OP.mult))
        b.wait("T", ev)
        stage = b.hvb(2 * (A0 + 8704 + 2 * width) + 2 * width + 256, KOT * 128)
        for t in range(KOT):
            pe_transpose(b, bh.rearrange("p (a c) -> p a c", c=128)[:, t, :],
                         stage[:, t * 128:(t + 1) * 128], 128, 128, BF16)
        ev = b.ev("V")
        b.wait("DY", ev)
        e3 = b.dma(dst_slabs[c], stage)
        b.wait("DY", e3)
    b.barrier()


def emit_l3(b, W, xin, xout):
    wq = b.inp("l3_wq", W["l3_wq"])
    wk = b.inp("l3_wk", W["l3_wk"])
    wv = b.inp("l3_wv", W["l3_wv"])
    wo = b.inp("l3_wo", W["l3_wo"])
    wgu = b.inp("l3_wgu", W["l3_wgu"])
    wpr = b.inp("l3_wpr", W["l3_wpr"])
    rc, rs, rp = W["l3_rope"]
    ropec = b.inp("l3_ropec", rc)
    ropes = b.inp("l3_ropes", rs)
    ropeP = b.inp("l3_ropeP", rp)
    sb_P = b.scal(128, 128)
    b.dma(sb_P, ropeP[:, :])
    b.dma(b.rope[:, 0:S], ropec[:, :])
    e = b.dma(b.rope[:, S:2 * S], ropes[:, :])
    for k in "TVS":
        b.wait(k, e)

    ln_stats(b, xin, D, NDC, 1.0e-5, True, R_LNM, R_LNR)
    b.barrier()
    build_xn(b, xin, R_LNM, R_LNR, lambda c: b.hv(A0 + c * S, S))
    b.barrier()
    rhs = arena_rhs(b)

    def evac_rope(dst):
        def f(idx, fo, tt, ps, ev_mm):
            t0 = tt * TT
            t = b.hv(GEN, TT)
            b.wait("V", ev_mm)
            ev = b.inc("V", b.V.tensor_copy(t, ps[:, :TT]))
            b.wait("T", ev)
            i2, ps2 = b.ps_get()
            mm = b.T.matmul(ps2[:, :TT], sb_P, t, start=True, stop=True)
            ev2 = b.inc("T", mm)
            b.wait("V", ev2)
            bt = b.exp_t[tt % 2]
            if b.exp_ev[tt % 2] is not None:
                b.wait("V", b.exp_ev[tt % 2])
            b.V.tensor_tensor(t, t, b.rope[:, t0:t0 + TT], OP.mult)
            b.V.tensor_tensor(ps2[:, :TT], ps2[:, :TT], b.rope[:, S + t0:S + t0 + TT],
                              OP.mult)
            ev3 = b.inc("V", b.V.tensor_tensor(bt[:, :TT], t, ps2[:, :TT], OP.add))
            b.ps_done(i2, ev3)
            b.wait("DY", ev3)
            b.exp_ev[tt % 2] = b.dma(dst[fo * 128:(fo + 1) * 128, t0:t0 + TT],
                                     bt[:, :TT])
            return ev3
        return f

    proj(b, wq, NDC, range(16), rhs, evac_rope(b.qbuf[0:2048, :]))
    proj(b, wk, NDC, range(4), rhs, evac_rope(b.kdram))
    proj(b, wv, NDC, range(4), rhs, evac_hbm(b, b.vdram))
    b.barrier()

    kg = b.heap[0:64, A0:A0 + S]
    vg = b.heap[0:64, A0 + S:A0 + 2 * S]
    vt = b.hv(A0 + 2 * S, 1024)
    qh = b.heap[0:64, A0 + 3 * S:A0 + 4 * S]
    cur_g = [-1]

    def prep_head(h):
        g = h // 4
        if cur_g[0] != g:
            cur_g[0] = g
            b.dma(kg, b.kdram[g * 64:(g + 1) * 64, :])
            b.dma(vg, b.vdram[g * 64:(g + 1) * 64, :])
            for k in "TV":
                b.dwait(k)
            for kc in range(16):
                pe_transpose(b, vg[:, kc * 128:(kc + 1) * 128],
                             vt[:, kc * 64:(kc + 1) * 64], 64, 128)
            b.barrier()
        e = b.dma(qh, b.qbuf[h * 64:(h + 1) * 64, :])
        b.wait("T", e)

    attn_softmax(b, 32, kparts_fn=lambda h: [(kg, 64)],
                 qparts_fn=lambda h: [(qh, 64)], vt_fn=lambda h: vt, vd=64,
                 scale_ap=b.scal(435),
                 abuf_row_fn=lambda h: b.abuf[h * 64:(h + 1) * 64, :],
                 prep_head=prep_head)
    b.barrier()

    hbm_to_arena(b, b.abuf, NDC)
    proj(b, wo, NDC, range(NDC), rhs, evac_hbm(b, b.sdram))
    b.barrier()

    ternary_dequant(b, wgu, 128, 2048, 16, b.l3qg)
    ternary_dequant(b, wpr, 16, 8192, 64, b.l3qp)
    b.barrier()

    TW = 256
    XH, XL, HH, HL = 0, 4096, 8192, 24576
    for t8 in range(S // TW):
        t0 = t8 * TW
        build_xn(b, xin, R_LNM, R_LNR,
                 lambda c: b.hv(A0 + 20480, TW),
                 split=(lambda c: b.hvb(XH + c * TW, TW),
                        lambda c: b.hvb(XL + c * TW, TW)),
                 t0=t0, w=TW, mb_off=WK0, rb_off=WK1, xt_off=GEN)
        b.barrier()

        def rhs_hi(ko, _t0, w):
            return b.hvb(XH + ko * TW, w)

        def rhs_lo(ko, _t0, w):
            return b.hvb(XL + ko * TW, w)

        ist, pst = b.ps_reserve()
        t1 = b.hv(GEN, TW)
        h32 = b.hv(GEN + TW, TW)
        sq = b.hv(GEN + 2 * TW, TW)
        hev = [None]
        for j in range(64):
            ld1 = b.dma(b.wkb[0][:, 0:2048], b.l3qg[j])
            ld2 = b.dma(b.wkb[1][:, 0:2048], b.l3qg[64 + j])
            b.wait("T", ld1)
            ia, psA = b.ps_get()
            for p, rf in ((0, rhs_hi), (1, rhs_lo)):
                for ko in range(16):
                    mmA = b.T.matmul(psA[:, :TW], b.wkb[0][:, ko * 128:(ko + 1) * 128],
                                     rf(ko, 0, TW), start=(p == 0 and ko == 0),
                                     stop=(p == 1 and ko == 15))
            evA = b.inc("T", mmA)
            b.wait("T", ld2)
            ib2, psB = b.ps_get()
            for p, rf in ((0, rhs_hi), (1, rhs_lo)):
                for ko in range(16):
                    mmB = b.T.matmul(psB[:, :TW], b.wkb[1][:, ko * 128:(ko + 1) * 128],
                                     rf(ko, 0, TW), start=(p == 0 and ko == 0),
                                     stop=(p == 1 and ko == 15))
            evB = b.inc("T", mmB)
            b.wait("S", evA)
            if hev[0] is not None:
                b.wait("S", hev[0])
            ev = b.inc("S", b.SC.activation(t1, psA[:, :TW], AF.Silu))
            b.ps_done(ia, ev)
            b.wait("V", ev)
            b.wait("V", evB)
            ev = b.inc("V", b.V.tensor_tensor(h32, t1, psB[:, :TW], OP.mult))
            b.ps_done(ib2, ev)
            b.wait("S", ev)
            ev_s = b.inc("S", b.SC.activation(sq, h32, AF.Square))
            b.wait("T", ev_s)
            mm = b.T.matmul(pst[0:1, :TW], b.s_ones[:, 0:1], sq,
                            start=(j == 0), stop=(j == 63))
            b.inc("T", mm)
            hh = b.hvb(HH + j * TW, TW)
            hl = b.hvb(HL + j * TW, TW)
            b.V.tensor_copy(hh, h32)
            ev = b.inc("V", b.V.tensor_tensor(hl, h32, hh, OP.subtract))
            hev[0] = ev
            te = b.tick("T")
            b.wait("DY", te)
        b.wait("V", b.ev("T"))
        ev = b.inc("V", b.V.tensor_scalar(b.row(R_B, t0, TW), pst[0:1, :TW],
                                          1.0 / 8192.0, 2.220446049250313e-16,
                                          OP.mult, OP.add))
        b.ps_release(ist, ev)
        b.wait("S", ev)
        ev = b.inc("S", b.SC.activation(b.row(R_C, t0, TW), b.row(R_B, t0, TW), AF.Sqrt))
        b.wait("V", ev)
        ev = b.inc("V", b.V.reciprocal(b.row(R_B, t0, TW), b.row(R_C, t0, TW)))
        b.wait("T", ev)
        rhb = b.hv(GEN + 3 * TW, TW)
        bcast_row(b, b.row(R_B, t0, TW), rhb, width=TW)
        b.barrier()

        for fo in range(16):
            i, ps = b.ps_get()
            mm = None
            for half in range(2):
                ld = b.dma(b.wkb[half][:, 0:4096],
                           b.l3qp[fo][:, half * 4096:(half + 1) * 4096])
                b.wait("T", ld)
                for p in range(2):
                    for j in range(32):
                        ko = half * 32 + j
                        rf = (b.hvb(HH + ko * TW, TW) if p == 0
                              else b.hvb(HL + ko * TW, TW))
                        mm = b.T.matmul(ps[:, :TW], b.wkb[half][:, j * 128:(j + 1) * 128],
                                        rf, start=(half == 0 and p == 0 and j == 0),
                                        stop=(half == 1 and p == 1 and j == 31))
            ev_mm = b.inc("T", mm)
            bt = b.exp_t[fo % 2]
            if b.exp_ev[fo % 2] is not None:
                b.wait("V", b.exp_ev[fo % 2])
            b.wait("V", ev_mm)
            ev = b.inc("V", b.V.tensor_tensor(bt[:, :TW], ps[:, :TW], rhb, OP.mult))
            b.ps_done(i, ev)
            b.wait("S", ev)
            ev = b.inc("S", b.SC.activation(bt[:, :TW], bt[:, :TW], AF.Silu))
            b.wait("DY", ev)
            b.exp_ev[fo % 2] = b.dma(b.gdram[fo * 128:(fo + 1) * 128, t0:t0 + TW],
                                     bt[:, :TW])
        b.barrier()
    combine(b, xin, xout)


EMITTERS[3] = emit_l3


def build(W, layers):
    b = Bld()
    nc = b.nc
    b.heap = b.sb("heap", [128, HEAP], F32)
    b.heapb = b.heap.bitcast(BF16)
    b.small = b.sb("small", [128, 8704], F32)
    b.rope = b.sb("rope", [128, 2 * S], F32)
    b.exp_t = [b.sb(f"expt{i}", [128, TT], F32) for i in range(2)]
    b.exp_ev = [None, None]
    b.s_ones = b.sb("ones", [128, 128], F32)
    b.s_id = b.sb("ident", [128, 128], F32)
    b.s_idb = b.sb("identb", [128, 128], BF16)
    b.s_mask = b.sb("mask", [128, 128], F32)
    b.wk = [b.hv(WK0, 2048), b.hv(WK1, 2048)]
    b.wkb = [b.hvb(2 * WK0, 5632), b.hvb(2 * WK0 + 5632, 5632)]

    c_ones = b.inp("c_ones", np.ones((128, 128), np.float32))
    c_id = b.inp("c_id", np.eye(128, dtype=np.float32))
    import ml_dtypes
    c_idb = b.inp("c_idb", np.eye(128).astype(ml_dtypes.bfloat16))
    m = np.zeros((128, 128), np.float32)
    for kk in range(128):
        m[kk, :kk] = -1.0e30
    c_mask = b.inp("c_mask", m)
    catt = np.zeros((128, 8), np.float32)
    catt[:, 0] = -12.0
    catt[:, 1] = float(192) ** -0.5
    catt[:, 2] = float(128) ** -0.5
    catt[:, 3] = float(64) ** -0.5
    c_att = b.inp("c_att", catt)
    b.dma(b.scal(432, 8), c_att[:, :])
    b.dma(b.s_ones[:, :], c_ones[:, :])
    b.dma(b.s_id[:, :], c_id[:, :])
    b.dma(b.s_idb[:, :], c_idb[:, :])
    e = b.dma(b.s_mask[:, :], c_mask[:, :])
    for k in "TVSG":
        b.wait(k, e)

    b.qbuf = b.dram("qbuf", [4608, S], F32)
    b.abuf = b.dram("abuf", [D, S], F32)
    b.sdram = b.dram("sdram", [D, S], F32)
    b.gdram = b.dram("gdram", [D, S], F32)
    b.kdram = b.dram("kdram", [640, S], F32)
    b.vdram = b.dram("vdram", [512, S], F32)
    b.hdram = b.dram("hdram", [8192, S], F32)
    b.s1flat = b.dram("s1flat", [180224], F32)
    b.deqflat = b.dram("deqflat", [180224], F32)
    if 0 in layers:
        b.w0q1 = b.dram("w0q1", [44, 128, 16 * 128], BF16)
        b.w0q2 = b.dram("w0q2", [44, 128, 16 * 128], BF16)
        b.w0q3 = b.dram("w0q3", [16, 128, 44 * 128], BF16)
    if 2 in layers:
        b.l2q1 = b.dram("l2q1", [44, 128, 16 * 128], BF16)
        b.l2q2 = b.dram("l2q2", [44, 128, 16 * 128], BF16)
        b.l2q3 = b.dram("l2q3", [16, 128, 44 * 128], BF16)
    if 3 in layers:
        b.l3qg = b.dram("l3qg", [128, 128, 16 * 128], BF16)
        b.l3qp = b.dram("l3qp", [16, 128, 64 * 128], BF16)

    xA = b.dram("xA", [D, S], F32)
    xB = b.dram("xB", [D, S], F32)
    x0 = b.inp_pc("x0", W["x0_cores"])
    yout = nc.dram_tensor("y", [D, S], F32, kind="ExternalOutput").ap()

    cur = x0
    ping = [xA, xB]
    for i, li in enumerate(layers):
        nxt = yout if i == len(layers) - 1 else ping[i % 2]
        EMITTERS[li](b, W, cur, nxt)
        cur = nxt
    for q in ("DY", "DG", "DS"):
        for j in range(4):
            k = q + str(j)
            if b.cnt[k] > 0:
                b.Y.wait_ge(b.sem[k], b.cnt[k])
                b.G.wait_ge(b.sem[k], b.cnt[k])
    return b


def prep_weights(inputs, layers=(0, 1, 2, 3)):
    W = {}
    x = np.asarray(inputs["L0_attn_hidden_states"])
    W["x0_cores"] = [np.ascontiguousarray(x[i].T) for i in range(8)]
    if 0 in layers:
        W["l0_qa"] = pack_slabs(np.asarray(inputs["L0_attn_q_a_w"]).T)
        W["l0_qb"] = pack_slabs(np.asarray(inputs["L0_attn_q_b_w"]).T)
        kv = np.zeros((2048, 640), np.float32)
        kv[:, :576] = np.asarray(inputs["L0_attn_kv_a_w"]).T
        W["l0_kv"] = pack_slabs(kv)
        W["l0_kb"] = pack_slabs(np.asarray(inputs["L0_attn_kv_b_w"]).T)
        W["l0_o"] = pack_slabs(np.asarray(inputs["L0_attn_o_w"]).T)
        W["l0_qa_ln"] = np.asarray(inputs["L0_attn_q_a_ln_w"])
        W["l0_kv_ln"] = np.asarray(inputs["L0_attn_kv_a_ln_w"])
        W["l0_w1"] = np.asarray(inputs["L0_ffn_weight1"])
        W["l0_w2"] = np.asarray(inputs["L0_ffn_weight2"])
        W["l0_w3"] = np.asarray(inputs["L0_ffn_weight3"])
        W["l0_rope"] = rope_neox(64, S)
    if 1 in layers:
        W["l1_wq"] = pack_slabs(np.asarray(inputs["L1_attn_wq"]).T)
        W["l1_wk"] = pack_slabs(np.asarray(inputs["L1_attn_wk"]).T)
        W["l1_wv"] = pack_slabs(np.asarray(inputs["L1_attn_wv"]).T)
        W["l1_wo"] = pack_slabs(np.asarray(inputs["L1_attn_wo"]).T)
        W["l1_w1"] = pack_slabs(np.asarray(inputs["L1_ffn_W1"]))
        W["l1_w2"] = pack_slabs(np.asarray(inputs["L1_ffn_W2"]))
        W["l1_wr"] = np.asarray(inputs["L1_ffn_W_r"]).astype(np.float32)
        W["l1_br"] = np.asarray(inputs["L1_ffn_b_r"]).astype(np.float32)
        W["l1_b1"] = np.asarray(inputs["L1_ffn_b1"])
        W["l1_b2"] = np.asarray(inputs["L1_ffn_b2"])
        W["l1_rope"] = rope_neox(128, S)
        lam = np.exp2(-8.0 * np.arange(1, 17, dtype=np.float32) / 16.0).astype(np.float64)
        r = np.arange(64, dtype=np.float64)
        diag = np.zeros((64, 16 * 64), np.float32)
        qdec = np.zeros((128, 16 * 64), np.float32)
        kdec = np.zeros((64, 16), np.float32)
        for h in range(16):
            dd = np.exp(-lam[h] * (r[None, :] - r[:, None]))
            dd = np.where(r[None, :] >= r[:, None], dd, 0.0)
            diag[:, h * 64:(h + 1) * 64] = dd.astype(np.float32)
            qdec[:, h * 64:(h + 1) * 64] = np.exp(-lam[h] * (r + 1.0)).astype(np.float32)[None, :]
            kdec[:, h] = np.exp(-lam[h] * (63.0 - r)).astype(np.float32)
        W["l1_diagT"] = diag
        W["l1_qdec"] = qdec
        W["l1_kdec"] = kdec
    if 2 in layers:
        W["l2_wq"] = pack_slabs(np.asarray(inputs["L2_attn_Wq"]))
        W["l2_wk"] = pack_slabs(np.asarray(inputs["L2_attn_Wk"]))
        W["l2_wv"] = pack_slabs(np.asarray(inputs["L2_attn_Wv"]))
        W["l2_wo"] = pack_slabs(np.asarray(inputs["L2_attn_Wo"]))
        W["l2_bo"] = np.asarray(inputs["L2_attn_bo"])
        W["l2_w1"] = pack_slabs(np.asarray(inputs["L2_ffn_w1"]).T)
        W["l2_w2"] = pack_slabs(np.asarray(inputs["L2_ffn_w2"]).T)
        W["l2_w3"] = pack_slabs(np.asarray(inputs["L2_ffn_w3"]).T)
    if 3 in layers:
        W["l3_wq"] = pack_slabs(np.asarray(inputs["L3_attn_Wq"]))
        W["l3_wk"] = pack_slabs(np.asarray(inputs["L3_attn_Wk"]))
        W["l3_wv"] = pack_slabs(np.asarray(inputs["L3_attn_Wv"]))
        W["l3_wo"] = pack_slabs(np.asarray(inputs["L3_attn_Wo"]))
        W["l3_wgu"] = np.asarray(inputs["L3_ffn_w_gate_up"])
        W["l3_wpr"] = np.asarray(inputs["L3_ffn_w_proj"])
        rc, rs, rp = rope_inter(64, S)
        rc2 = np.vstack([rc, rc])
        rs2 = np.vstack([rs, rs])
        rp2 = np.zeros((128, 128), np.float32)
        rp2[0:64, 0:64] = rp
        rp2[64:128, 64:128] = rp
        W["l3_rope"] = (rc2, rs2, rp2)
    return W


def kernel(_layers=(0, 1, 2, 3), **inputs):
    W = prep_weights(inputs, _layers)
    b = build(W, list(_layers))
    in_maps = []
    for c in range(8):
        m = dict(b.inputs)
        for k, arrs in b.per_core.items():
            m[k] = arrs[c]
        in_maps.append(m)
    res = run_bass_kernel_spmd(b.nc, in_maps, core_ids=list(range(8)))
    out = np.stack([np.ascontiguousarray(res.results[c]["y"].T) for c in range(8)])
    b.ctx.close()
    return out
